# version 01
# baseline (speedup 1.0000x reference)
import contextlib
import numpy as np
import concourse.bass as bass
import concourse.mybir as mybir
from concourse.bass_utils import run_bass_kernel_spmd

F32 = mybir.dt.float32
BF16 = mybir.dt.bfloat16
I32 = mybir.dt.int32
AF = mybir.ActivationFunctionType
ALU = mybir.AluOpType
AX = mybir.AxisListType
PI = float(np.pi)
MAGIC = 12582912.0

NCORES = 8
B, L, D = 4, 4096, 1024
HALF = L // 2


class Sched:
    ENGS = ("sync", "vector", "scalar", "tensor", "gpsimd")

    def __init__(self):
        self.steps = []
        self.dep_targets = set()
        self._auto = 0

    def add(self, engine, emit, deps=(), dma=False, stream=None):
        sid = len(self.steps)
        deps = [d for d in deps if d is not None]
        if dma and stream is None:
            stream = f"_auto{self._auto}"
            self._auto += 1
        self.steps.append(dict(engine=engine, emit=emit, deps=tuple(deps),
                               dma=dma, stream=stream))
        for d in deps:
            self.dep_targets.add(d)
        return sid

    def dma(self, queue, out, in_, deps=(), stream=None):
        return self.add(queue, lambda e, o=out, i=in_: e.dma_start(out=o, in_=i),
                        deps=deps, dma=True, stream=stream)

    def emit(self, nc, es):
        marked = [sid in self.dep_targets for sid in range(len(self.steps))]
        last_of = {}
        for sid, st in enumerate(self.steps):
            last_of[st["engine"]] = sid
        for sid in last_of.values():
            marked[sid] = True
        for sid, st in enumerate(self.steps):
            if st["dma"]:
                marked[sid] = True
        def key_of(st):
            return ("dma", st["stream"]) if st["dma"] else ("eng", st["engine"])
        count_after = {}
        run = {}
        for sid, st in enumerate(self.steps):
            k = key_of(st)
            if marked[sid]:
                run[k] = run.get(k, 0) + (16 if st["dma"] else 1)
            count_after[sid] = (k, run.get(k, 0))
        sems = {k: es.enter_context(nc.semaphore(f"s_{k[1]}" if k[0] == "eng"
                                                 else f"d_{k[1]}"))
                for k in run}
        self._sems, self._run = sems, run

        def section(engine):
            def body(eng):
                waited = {}
                for sid, st in enumerate(self.steps):
                    if st["engine"] != engine:
                        continue
                    need = {}
                    for d in st["deps"]:
                        dk, dcnt = count_after[d]
                        if (dk == ("eng", "tensor") and engine == "tensor"
                                and not st["dma"]):
                            continue
                        need[dk] = max(need.get(dk, 0), dcnt)
                    for dk, dcnt in need.items():
                        if waited.get(dk, -1) >= dcnt:
                            continue
                        eng.wait_ge(sems[dk], dcnt)
                        waited[dk] = dcnt
                    instr = st["emit"](eng)
                    if marked[sid]:
                        k = key_of(st)
                        instr.then_inc(sems[k], 16 if st["dma"] else 1)
            return body

        return section

    def finalize(self, nc, es, block, tail_engine="gpsimd"):
        section = self.emit(nc, es)

        def tail_body(eng):
            for k, cnt in self._run.items():
                if k == ("eng", tail_engine):
                    continue
                eng.wait_ge(self._sems[k], cnt)

        secs = {
            "sync": block.sync, "vector": block.vector, "scalar": block.scalar,
            "tensor": block.tensor, "gpsimd": block.gpsimd,
        }
        for e in self.ENGS:
            base = section(e)
            if e == tail_engine:
                def combined(eng, base=base):
                    base(eng)
                    tail_body(eng)
                secs[e](combined)
            else:
                if any(st["engine"] == e for st in self.steps):
                    secs[e](base)


class TileCtx:
    def __init__(self, nc, es):
        self.nc = nc
        self.es = es
        self.n = 0

    def sbuf(self, name, shape, dtype):
        return self.es.enter_context(self.nc.sbuf_tensor(name, shape, dtype))

    def psum(self, name, shape, dtype=F32):
        return self.es.enter_context(self.nc.psum_tensor(name, shape, dtype))


def emit_ln_tile(S, tc, x_t, xn_t, scratch, cols, eps_col, inv_d, d_load,
                 extra_deps=()):
    Dm = x_t.shape[-1]
    deps = [d_load] + list(extra_deps)
    s_sum = S.add("vector", lambda e: e.reduce_sum(cols[:, 0:1], x_t[:], axis=AX.X),
                  deps=deps)
    s_mean = S.add("vector", lambda e: e.tensor_scalar_mul(cols[:, 0:1], cols[:, 0:1], inv_d),
                   deps=[s_sum])
    s_sq = S.add("scalar", lambda e: e.activation(out=scratch[:], in_=x_t[:], func=AF.Square,
                                                  accum_out=cols[:, 1:2]), deps=deps)
    s_m2 = S.add("vector", lambda e: e.tensor_tensor(cols[:, 2:3], cols[:, 0:1], cols[:, 0:1],
                                                     ALU.mult), deps=[s_mean])
    s_var = S.add("vector", lambda e: e.scalar_tensor_tensor(cols[:, 1:2], cols[:, 1:2], inv_d,
                                                             cols[:, 2:3], ALU.mult, ALU.subtract),
                  deps=[s_sq, s_m2])
    s_ln = S.add("scalar", lambda e: e.activation(out=cols[:, 1:2], in_=cols[:, 1:2], func=AF.Ln,
                                                  bias=eps_col, scale=1.0), deps=[s_var])
    s_ex = S.add("scalar", lambda e: e.activation(out=cols[:, 1:2], in_=cols[:, 1:2], func=AF.Exp,
                                                  scale=-0.5), deps=[s_ln])
    s_xn = S.add("vector", lambda e: e.tensor_scalar(xn_t[:], x_t[:], cols[:, 0:1], cols[:, 1:2],
                                                     op0=ALU.subtract, op1=ALU.mult),
                 deps=[s_ex])
    return s_xn


def build_ln0():
    nc = bass.Bass("TRN2", target_bir_lowering=False)
    x = nc.dram_tensor("x", [HALF, D], F32, kind="ExternalInput").ap()
    xn = nc.dram_tensor("xn", [HALF, D], F32, kind="ExternalOutput").ap()
    es = contextlib.ExitStack()
    with es:
        tc = TileCtx(nc, es)
        S = Sched()
        NT = HALF // 128
        xt = [tc.sbuf(f"xt{i}", [128, D], F32) for i in range(2)]
        xnt = [tc.sbuf(f"xnt{i}", [128, D], F32) for i in range(2)]
        scratch = tc.sbuf("scr", [128, D], F32)
        cols = tc.sbuf("cols", [128, 4], F32)
        epsc = tc.sbuf("epsc", [128, 1], F32)
        s_eps = S.add("vector", lambda e: e.memset(epsc[:], 1e-5))
        xr = x.rearrange("(n p) f -> n p f", p=128)
        xnr = xn.rearrange("(n p) f -> n p f", p=128)
        prev_store = {}
        prev_ln = None
        with nc.Block() as block:
            for i in range(NT):
                b = i % 2
                d_ld = S.dma("sync", xt[b][:], xr[i, :, :],
                             deps=[prev_store.get(b)], stream=f"x{b}")
                s_xn = emit_ln_tile(S, tc, xt[b], xnt[b], scratch, cols, epsc[:, 0:1],
                                    1.0 / D, d_ld,
                                    extra_deps=[s_eps, prev_ln, prev_store.get(b)])
                prev_ln = s_xn
                prev_store[b] = S.dma("gpsimd", xnr[i, :, :], xnt[b][:], deps=[s_xn],
                                      stream=f"st{b}")
            S.finalize(nc, es, block)
    return nc


def emit_transpose_to_fm(S, src_tm, dst_fm_chunks, col_off, ident, ps_tr, deps):
    steps = []
    prev = list(deps)
    for dc in range(len(dst_fm_chunks)):
        d_tr = S.add("tensor", lambda e, dc=dc: e.transpose(
            ps_tr[:, 0:128], src_tm[:, dc * 128:(dc + 1) * 128], ident[:]),
            deps=prev)
        d_ev = S.add("scalar", lambda e, dc=dc: e.copy(
            out=dst_fm_chunks[dc][:, col_off:col_off + 128], in_=ps_tr[:, 0:128]),
            deps=[d_tr])
        steps.append(d_ev)
        prev = [d_ev]
    return steps


def emit_topk_combine(S, logits, logits2, c_out, m8, colbuf, kk, d_logits):
    d_m8 = S.add("vector", lambda e: e.max(out=m8[:], in_=logits[:]), deps=[d_logits])
    d_nm = S.add("vector", lambda e: e.tensor_scalar_mul(colbuf[:, 0:1], m8[:, 0:1], -1.0),
                 deps=[d_m8])
    d_mask = S.add("vector", lambda e: e.tensor_scalar(c_out[:], logits[:], m8[:, kk - 1:kk],
                                                       None, op0=ALU.is_ge), deps=[d_m8])
    d_exp = S.add("scalar", lambda e: e.activation(out=logits2[:], in_=logits[:], func=AF.Exp,
                                                   bias=colbuf[:, 0:1], scale=1.0),
                  deps=[d_nm, d_mask])
    d_cu = S.add("vector", lambda e: e.tensor_tensor(c_out[:], c_out[:], logits2[:], ALU.mult),
                 deps=[d_exp])
    d_sum = S.add("vector", lambda e: e.reduce_sum(colbuf[:, 1:2], c_out[:], axis=AX.X),
                  deps=[d_cu])
    d_rcp = S.add("vector", lambda e: e.reciprocal(colbuf[:, 1:2], colbuf[:, 1:2]),
                  deps=[d_sum])
    d_c = S.add("vector", lambda e: e.tensor_scalar_mul(c_out[:], c_out[:], colbuf[:, 1:2]),
                deps=[d_rcp])
    return d_c, d_exp


def build_moe(E, KK, act_name, residual_init, dbg=False):
    act_func = getattr(AF, act_name)
    two_layer = (E == 8)
    nc = bass.Bass("TRN2", target_bir_lowering=False)
    xn = nc.dram_tensor("xn", [HALF, D], F32, kind="ExternalInput").ap()
    ident_d = nc.dram_tensor("ident", [128, 128], F32, kind="ExternalInput").ap()
    w1_d = nc.dram_tensor("w1", [E, D, 2048], F32, kind="ExternalInput").ap()
    b1_d = nc.dram_tensor("b1", [E, 2048], F32, kind="ExternalInput").ap()
    w2_d = nc.dram_tensor("w2", [E, 2048, D], F32, kind="ExternalInput").ap()
    b2_d = nc.dram_tensor("b2", [E, D], F32, kind="ExternalInput").ap()
    if two_layer:
        rw1_d = nc.dram_tensor("rw1", [D, D], F32, kind="ExternalInput").ap()
        rb1_d = nc.dram_tensor("rb1", [1, D], F32, kind="ExternalInput").ap()
        rw_d = nc.dram_tensor("rw2", [D, E], F32, kind="ExternalInput").ap()
        rb_d = nc.dram_tensor("rb2", [1, E], F32, kind="ExternalInput").ap()
    else:
        rw_d = nc.dram_tensor("rw", [D, E], F32, kind="ExternalInput").ap()
        rb_d = nc.dram_tensor("rb", [1, E], F32, kind="ExternalInput").ap()
    out_d = nc.dram_tensor("out", [HALF, D], F32, kind="ExternalOutput").ap()
    if dbg:
        dbg_c = nc.dram_tensor("dbg_c", [128, 16 * E], F32, kind="ExternalOutput").ap()
        dbg_cfm = nc.dram_tensor("dbg_cfm", [E, HALF], F32, kind="ExternalOutput").ap()
        dbg_rb = nc.dram_tensor("dbg_rb", [128, E], F32, kind="ExternalOutput").ap()
        dbg_xnf = nc.dram_tensor("dbg_xnf", [128, HALF], F32, kind="ExternalOutput").ap()
        dbg_lg = nc.dram_tensor("dbg_lg", [128, E], F32, kind="ExternalOutput").ap()
        dbg_mid = nc.dram_tensor("dbg_mid", [128, 8 * 512], F32, kind="ExternalOutput").ap()
        dbg_hm = nc.dram_tensor("dbg_hm", [128, 16 * 1024], F32, kind="ExternalOutput").ap()

    NDC = D // 128
    NTT = HALF // 128
    NIC = 2048 // 128
    W1P, W2P = 8, 8
    P1W = 2048 // W1P
    C1 = NIC // W1P
    C2 = NIC // W2P
    NW1B, NW2B = (10, 10) if E == 16 else (9, 8)

    es = contextlib.ExitStack()
    with es:
        tc = TileCtx(nc, es)
        S = Sched()
        xnf = [tc.sbuf(f"xnf{dc}", [128, HALF], BF16) for dc in range(NDC)]
        ident = tc.sbuf("identt", [128, 128], F32)
        hm = tc.sbuf("hm", [128, NIC, 1024], BF16)
        acc = tc.sbuf("acc", [128, 8, 1024], F32)
        c_all = tc.sbuf("c_all", [128, NTT, E], F32)
        cfm = tc.sbuf("cfm", [E, HALF], BF16)
        m8 = tc.sbuf("m8", [128, 8], F32)
        colbuf = tc.sbuf("colbuf", [128, 4], F32)
        logits = tc.sbuf("logits", [128, E], F32)
        logits2 = tc.sbuf("logits2", [128, E], F32)
        rbias = tc.sbuf("rbias", [128, E], F32)
        onesr = tc.sbuf("onesr", [1, 128], F32)
        b2t = tc.sbuf("b2t", [E, D], BF16)
        b1t = [tc.sbuf(f"b1t{j}", [128, NIC], F32) for j in range(2)]
        b1raw = [tc.sbuf(f"b1raw{j}", [NIC, 128], F32) for j in range(2)]
        xt = [tc.sbuf(f"xt{j}", [128, D], F32) for j in range(2)]
        rwt = tc.sbuf("rwt", [128, NDC, E], BF16)
        if two_layer:
            rw1t = tc.sbuf("rw1t", [128, NDC, D], BF16)
            rb1t = tc.sbuf("rb1t", [128, NDC], F32)
            rb1raw = tc.sbuf("rb1raw", [NDC, 128], F32)
            midf = tc.sbuf("midf", [128, NDC, 512], BF16)
        wb1 = [tc.sbuf(f"wb1_{j}", [128, NDC, P1W], BF16) for j in range(NW1B)]
        wb2 = [tc.sbuf(f"wb2_{j}", [128, C2, D], BF16) for j in range(NW2B)]
        ps_tr = tc.psum("ps_tr", [128, 128], F32)
        ps_r = tc.psum("ps_r", [128, 512], F32)
        ps_h = [tc.psum(f"ps_h{j}", [128, 512], F32) for j in range(2)]
        ps_o = [tc.psum(f"ps_o{j}", [128, 512], F32) for j in range(2)]
        ps_bc = tc.psum("ps_bc", [128, 512], F32)

        xr = xn.rearrange("(n p) f -> n p f", p=128)
        outr = out_d.rearrange("(n p) f -> n p f", p=128)

        with nc.Block() as block:
            d_id = S.dma("sync", ident[:], ident_d[:, :])
            d_ones = S.add("vector", lambda e: e.memset(onesr[:], 1.0))
            d_b2 = S.dma("gpsimd", b2t[:], b2_d[:, :])
            d_rw = S.dma("gpsimd", rwt[:], rw_d.rearrange("(c p) f -> p c f", p=128))
            d_rb = S.dma("sync", rbias[0:1, :], rb_d[:, :])
            router_deps = [d_rw]
            if two_layer:
                d_rw1 = S.dma("gpsimd", rw1t[:], rw1_d.rearrange("(c p) f -> p c f", p=128))
                d_rb1r = S.dma("sync", rb1raw[:], rb1_d.rearrange("o (c p) -> (o c) p", p=128))
                router_deps += [d_rw1]
            d_bcr = S.add("tensor", lambda e: e.matmul(ps_r[:, 0:E], onesr[:], rbias[0:1, :],
                                                       start=True, stop=True),
                          deps=[d_ones, d_rb, d_id])
            d_rbias = S.add("vector", lambda e: e.tensor_copy(rbias[:], ps_r[:, 0:E]),
                            deps=[d_bcr])
            if two_layer:
                d_rb1tr = S.add("tensor", lambda e: e.transpose(
                    ps_tr[:, 0:NDC], rb1raw[:], ident[0:NDC, 0:NDC]), deps=[d_rb1r, d_id])
                d_rb1 = S.add("vector", lambda e: e.tensor_copy(rb1t[:], ps_tr[:, 0:NDC]),
                              deps=[d_rb1tr])
                router_deps += [d_rb1]
                tr0_extra = [d_rb1]
            else:
                tr0_extra = []

            ev_last = None
            prev_tr = d_bcr
            prev_ld = {0: None, 1: None}
            for tt in range(NTT):
                bb = tt % 2
                d_x = S.dma("sync", xt[bb][:], xr[tt, :, :], deps=[prev_ld[bb]],
                            stream=f"xt{bb}")
                ev = emit_transpose_to_fm(S, xt[bb], xnf, tt * 128, ident, ps_tr,
                                          [d_x, d_id, prev_tr] + tr0_extra)
                tr0_extra = []
                prev_tr = ev[-1]
                prev_ld[bb] = ev[-1]
            ev_last = prev_tr

            last_c = ev_last
            d_exp_prev = None
            d_lg_prev = None
            prev_sl = None
            sl_war = {0: None, 1: None}
            for sp in range(NTT // 4):
                if two_layer:
                    d_sl = []
                    for mc in range(NDC):
                        pb = mc % 2
                        mm = None
                        for dc in range(NDC):
                            mm = S.add("tensor", lambda e, mc=mc, dc=dc, sp=sp, pb=pb: e.matmul(
                                ps_h[pb][:, 0:512],
                                rw1t[:, dc, mc * 128:(mc + 1) * 128],
                                xnf[dc][:, sp * 512:(sp + 1) * 512],
                                start=(dc == 0), stop=(dc == NDC - 1)),
                                deps=([ev_last, sl_war[pb]] + router_deps) if dc == 0 else [])
                        d_sl.append(S.add("scalar", lambda e, mc=mc, pb=pb: e.activation(
                            out=midf[:, mc, :], in_=ps_h[pb][:, 0:512],
                            func=AF.Silu, bias=rb1t[:, mc:mc + 1], scale=1.0),
                            deps=[mm, prev_sl]))
                        sl_war[pb] = d_sl[-1]
                    prev_sl = d_sl[-1]
                    if dbg and sp == 3:
                        d_dm = S.dma("gpsimd", dbg_mid[:, :], midf[:, :, :], deps=[d_sl[-1]])
                        sl_war = {0: d_dm, 1: d_dm}
                for ti in range(4):
                    tt = sp * 4 + ti
                    mm = None
                    for dc in range(NDC):
                        if two_layer:
                            lhsT = (lambda e, dc=dc, ti=ti: midf[:, dc, ti * 128:(ti + 1) * 128])
                        else:
                            lhsT = (lambda e, dc=dc, tt=tt: xnf[dc][:, tt * 128:(tt + 1) * 128])
                        dep0 = ([ev_last, d_rbias, d_lg_prev] + router_deps
                                + ([d_sl[-1]] if two_layer else []))
                        mm = S.add("tensor", lambda e, dc=dc, lhsT=lhsT: e.matmul(
                            ps_r[:, 0:E], lhsT(e), rwt[:, dc, :],
                            start=(dc == 0), stop=(dc == NDC - 1)),
                            deps=dep0 if dc == 0 else [])
                    d_lg = S.add("vector", lambda e: e.tensor_tensor(
                        logits[:], ps_r[:, 0:E], rbias[:], ALU.add),
                        deps=[mm, d_exp_prev])
                    d_lg_prev = d_lg
                    d_c, d_exp_prev = emit_topk_combine(S, logits, logits2, c_all[:, tt, :],
                                                        m8, colbuf, KK, d_lg)
                    d_trc = S.add("tensor", lambda e, tt=tt: e.transpose(
                        ps_tr[0:E, 0:128], c_all[:, tt, :], ident[:]), deps=[d_c, prev_tr])
                    d_cf = S.add("scalar", lambda e, tt=tt: e.copy(
                        out=cfm[:, tt * 128:(tt + 1) * 128], in_=ps_tr[0:E, 0:128]),
                        deps=[d_trc])
                    prev_tr = d_cf
                    last_c = d_cf

            if dbg:
                dbx = tc.sbuf("dbx", [128, HALF], F32)
                d_cv = S.add("vector", lambda e: e.tensor_copy(dbx[:, 0:HALF], xnf[0][:]),
                             deps=[ev_last])
                S.dma("gpsimd", dbg_xnf[:, :], dbx[:], deps=[d_cv])
                S.dma("gpsimd", dbg_c[:, :], c_all[:, :, :].rearrange("p a b -> p (a b)")
                      if False else c_all[:, :, :], deps=[last_c])
                S.dma("gpsimd", dbg_cfm[:, :], cfm[:], deps=[last_c])
                S.dma("gpsimd", dbg_rb[:, :], rbias[:], deps=[last_c])
                S.dma("gpsimd", dbg_lg[:, :], logits[:], deps=[last_c])

            w1_src = w1_d.rearrange("e (c p) f -> e p c f", p=128)
            w2_src = w2_d.rearrange("e (c p) f -> e p c f", p=128)
            b1_src = b1_d.rearrange("e (c p) -> e c p", p=128)
            wb1_war = {j: None for j in range(NW1B)}
            ps_tr_war = [last_c]
            wb2_war = {j: None for j in range(NW2B)}
            b1_war = {0: None, 1: None}
            d_stores = []
            prev_gelu_war = dict(sl_war) if two_layer else {0: last_c, 1: last_c}
            if two_layer:
                prev_gelu_war = {k: (v if v is not None else last_c)
                                 for k, v in prev_gelu_war.items()}
            prev_comb = {}
            for half in range(2):
                t0 = half * 1024
                d_acc_init = {}
                prev_bc = None
                for ttl in range(8):
                    tt = half * 8 + ttl
                    if residual_init:
                        d_xres = S.dma("sync", acc[:, ttl, :], xr[tt, :, :],
                                       deps=[prev_comb.get((ttl, 0)), prev_comb.get((ttl, 1))]
                                       + d_stores, stream=f"xres{ttl}")
                    for dh in range(2):
                        d_mm = S.add("tensor", lambda e, tt=tt, dh=dh: e.matmul(
                            ps_bc[:, 0:512], cfm[:, tt * 128:(tt + 1) * 128],
                            b2t[:, dh * 512:(dh + 1) * 512], start=True, stop=True),
                            deps=[last_c, d_b2, prev_bc] + d_stores)
                        d_stores = []
                        if residual_init:
                            prev_bc = S.add("vector", lambda e, ttl=ttl, dh=dh: e.tensor_tensor(
                                acc[:, ttl, dh * 512:(dh + 1) * 512], ps_bc[:, 0:512],
                                acc[:, ttl, dh * 512:(dh + 1) * 512], ALU.add),
                                deps=[d_mm, d_xres])
                        else:
                            prev_bc = S.add("vector", lambda e, ttl=ttl, dh=dh: e.tensor_copy(
                                acc[:, ttl, dh * 512:(dh + 1) * 512], ps_bc[:, 0:512]),
                                deps=[d_mm] + ([prev_comb.get((ttl, dh))] if prev_comb else []))
                        d_acc_init[(ttl, dh)] = prev_bc
                last_w2_mm_prev = None
                for e_i in range(E):
                    d_w1 = []
                    for p in range(W1P):
                        j = ((half * E + e_i) * W1P + p) % NW1B
                        d = S.dma("gpsimd", wb1[j][:],
                                  w1_src[e_i, :, :, p * P1W:(p + 1) * P1W],
                                  deps=[wb1_war[j]], stream=f"wb1_{j}")
                        d_w1.append((j, d))
                    d_w2 = []
                    for p in range(W2P):
                        j = ((half * E + e_i) * W2P + p) % NW2B
                        d = S.dma("gpsimd", wb2[j][:],
                                  w2_src[e_i, :, p * C2:(p + 1) * C2, :],
                                  deps=[wb2_war[j]], stream=f"wb2_{j}")
                        d_w2.append((j, d))
                    jb = e_i % 2
                    d_b1r = S.dma("sync", b1raw[jb][:], b1_src[e_i, :, :], deps=[b1_war[jb]],
                                  stream=f"b1raw{jb}")
                    d_b1tr = S.add("tensor", lambda e, jb=jb: e.transpose(
                        ps_tr[:, 0:NIC], b1raw[jb][:], ident[0:NIC, 0:NIC]),
                        deps=[d_b1r, ps_tr_war[0]])
                    d_b1 = S.add("vector", lambda e, jb=jb: e.tensor_copy(
                        b1t[jb][:], ps_tr[:, 0:NIC]), deps=[d_b1tr])
                    ps_tr_war[0] = d_b1
                    d_gelu = {}
                    for ic in range(NIC):
                        jbuf, d_ldw = d_w1[ic // C1]
                        jcol = ic % C1
                        for th in range(2):
                            pb = th
                            mmlast = None
                            for dc in range(NDC):
                                deps = []
                                if dc == 0:
                                    deps = [d_ldw, prev_gelu_war[pb],
                                            d_gelu.get((ic - 1, th)), last_c]
                                mmlast = S.add("tensor", lambda e, jbuf=jbuf, jcol=jcol, th=th, dc=dc, pb=pb, t0=t0: e.matmul(
                                    ps_h[pb][:, 0:512],
                                    wb1[jbuf][:, dc, jcol * 128:(jcol + 1) * 128],
                                    xnf[dc][:, t0 + th * 512:t0 + (th + 1) * 512],
                                    start=(dc == 0), stop=(dc == NDC - 1)), deps=deps)
                            d_gelu[(ic, th)] = S.add("scalar", lambda e, ic=ic, th=th, pb=pb, jb=jb: e.activation(
                                out=hm[:, ic, th * 512:(th + 1) * 512], in_=ps_h[pb][:, 0:512],
                                func=act_func, bias=b1t[jb][:, ic:ic + 1], scale=1.0),
                                deps=[mmlast, d_b1, last_w2_mm_prev])
                            wb1_war[jbuf] = d_gelu[(ic, th)]
                    b1_war[jb] = d_gelu[(NIC - 1, 1)]
                    prev_gelu_war = {0: d_gelu[(NIC - 1, 0)], 1: d_gelu[(NIC - 1, 1)]}
                    if dbg and half == 0 and e_i == 0:
                        d_dhm = S.dma("gpsimd", dbg_hm[:, :], hm[:, :, :],
                                      deps=[d_gelu[(NIC - 1, 0)], d_gelu[(NIC - 1, 1)]])
                        prev_gelu_war = {0: d_dhm, 1: d_dhm}
                    d_comb = {}
                    for ttl in range(8):
                        tt = half * 8 + ttl
                        for dh in range(2):
                            pb = dh
                            mmlast = None
                            for ic in range(NIC):
                                jbuf, d_ldw = d_w2[ic // C2]
                                jrow = ic % C2
                                deps = [d_ldw]
                                if ic == 0:
                                    deps += [d_gelu[(NIC - 1, 0)], d_gelu[(NIC - 1, 1)],
                                             d_comb.get((ttl - 1, dh),
                                                        prev_comb.get((7, dh)))]
                                mmlast = S.add("tensor", lambda e, ttl=ttl, dh=dh, jbuf=jbuf, jrow=jrow, ic=ic, pb=pb: e.matmul(
                                    ps_o[pb][:, 0:512],
                                    hm[:, ic, ttl * 128:(ttl + 1) * 128],
                                    wb2[jbuf][:, jrow, dh * 512:(dh + 1) * 512],
                                    start=(ic == 0), stop=(ic == NIC - 1)), deps=deps)
                                wb2_war[jbuf] = mmlast
                            d_comb[(ttl, dh)] = S.add("vector", lambda e, ttl=ttl, tt=tt, dh=dh, e_i=e_i, pb=pb: e.scalar_tensor_tensor(
                                out=acc[:, ttl, dh * 512:(dh + 1) * 512],
                                in0=ps_o[pb][:, 0:512],
                                scalar=c_all[:, tt, e_i:e_i + 1],
                                in1=acc[:, ttl, dh * 512:(dh + 1) * 512],
                                op0=ALU.mult, op1=ALU.add),
                                deps=[mmlast, d_acc_init[(ttl, dh)]])
                            last_w2_mm_prev = mmlast
                    prev_comb = d_comb
                d_stores = []
                for ttl in range(8):
                    tt = half * 8 + ttl
                    d_st = S.dma("gpsimd", outr[tt, :, :], acc[:, ttl, :],
                                 deps=[prev_comb[(ttl, 0)], prev_comb[(ttl, 1)]],
                                 stream="outst")
                    d_stores.append(d_st)
            S.finalize(nc, es, block)
    return nc


def build_l0attn():
    nc = bass.Bass("TRN2", target_bir_lowering=False)
    SEQ = L
    NT = SEQ // 512
    xn_d = nc.dram_tensor("xn", [SEQ, D], F32, kind="ExternalInput").ap()
    win_d = nc.dram_tensor("win", [D, 1040], F32, kind="ExternalInput").ap()
    ck_d = nc.dram_tensor("ck", [4, 1040], F32, kind="ExternalInput").ap()
    theta_d = nc.dram_tensor("theta", [128, 4], F32, kind="ExternalInput").ap()
    slope_d = nc.dram_tensor("slope", [16, 1], F32, kind="ExternalInput").ap()
    selrow_d = nc.dram_tensor("selrow", [2, 16], F32, kind="ExternalInput").ap()
    posdist_d = nc.dram_tensor("posdist", [2, SEQ], F32, kind="ExternalInput").ap()
    ss_d = nc.dram_tensor("ss", [16, 1], F32, kind="ExternalInput").ap()
    wre_d = nc.dram_tensor("wre", [128, 128], F32, kind="ExternalInput").ap()
    wim_d = nc.dram_tensor("wim", [128, 128], F32, kind="ExternalInput").ap()
    nsre_d = nc.dram_tensor("nsre", [128, 1], F32, kind="ExternalInput").ap()
    nsim_d = nc.dram_tensor("nsim", [128, 1], F32, kind="ExternalInput").ap()
    wout_d = nc.dram_tensor("wout", [512, D], F32, kind="ExternalInput").ap()
    b16_d = nc.dram_tensor("b16", [16, 512], F32, kind="ExternalInput").ap()
    bsum_d = nc.dram_tensor("bsum", [128, 64], F32, kind="ExternalInput").ap()
    ident_d = nc.dram_tensor("ident", [128, 128], F32, kind="ExternalInput").ap()
    out_d = nc.dram_tensor("out", [HALF, D], F32, kind="ExternalOutput").ap()

    s0p = nc.dram_tensor("s0p", [SEQ, D], F32)
    s0h = nc.dram_tensor("s0h", [HALF, D], F32)
    PAIRS = [[0, 1], [2, 3], [4, 5], [6, 7]]
    NDC = D // 128

    es = contextlib.ExitStack()
    with es:
        tc = TileCtx(nc, es)
        S = Sched()
        xnf = [tc.sbuf(f"xnf{dc}", [128, SEQ], BF16) for dc in range(NDC)]
        wint = tc.sbuf("wint", [128, NDC, 1040], BF16)
        woutt = tc.sbuf("woutt", [128, 4, D], BF16)
        wreb = tc.sbuf("wreb", [128, 128], BF16)
        wimb = tc.sbuf("wimb", [128, 128], BF16)
        identt = tc.sbuf("identt", [128, 128], F32)
        theta = tc.sbuf("thetat", [128, 4], F32)
        nsre = tc.sbuf("nsret", [128, 1], F32)
        nsim = tc.sbuf("nsimt", [128, 1], F32)
        b16 = tc.sbuf("b16t", [16, 512], BF16)
        bsum = tc.sbuf("bsumt", [128, 64], F32)
        posdist = tc.sbuf("posdistt", [2, 512], F32)
        selrow = tc.sbuf("selrowt", [2, 16], F32)
        sscol = tc.sbuf("sscol", [16, 1], F32)
        nspcol = tc.sbuf("nspcol", [16, 4], F32)
        ckraw = tc.sbuf("ckraw", [4, 1040], F32)
        ckt = tc.sbuf("ckt", [128, 9, 4], F32)
        xt = [tc.sbuf(f"xt{j}", [128, D], F32) for j in range(2)]
        zpre = tc.sbuf("zpre", [128, 9, 515], F32)
        zcv = tc.sbuf("zcv", [128, 9, 512], F32)
        csb = tc.sbuf("csb", [128, 8, 512], F32)
        nrni = tc.sbuf("nrni", [128, 8, 512], BF16)
        csd = tc.sbuf("csd", [16, 512], F32)
        pwt = tc.sbuf("pwt", [16, 4, 512], F32)
        pwbf = tc.sbuf("pwbf", [16, 512], BF16)
        wkbf = tc.sbuf("wkbf", [16, 512], BF16)
        rsb = tc.sbuf("rsb", [16, 512], F32)
        onescol = tc.sbuf("onescol", [128, 512], F32)
        carry = tc.sbuf("carry", [128, 9], F32)
        carryd = tc.sbuf("carryd", [16, 1], F32)
        tmp1 = tc.sbuf("tmp1", [128, 512], F32)
        tmp2 = tc.sbuf("tmp2", [128, 512], F32)
        tmp3 = tc.sbuf("tmp3", [128, 512], F32)
        cosv = tc.sbuf("cosv", [128, 512], F32)
        gatev = tc.sbuf("gatev", [128, 512], F32)
        opbf = tc.sbuf("opbf", [128, 2, 512], BF16)
        ygbf = tc.sbuf("ygbf", [128, 4, 512], BF16)
        sevac = tc.sbuf("sevac", [128, 1024], F32)
        picol = tc.sbuf("picol", [128, 2], F32)

        ps_tr = tc.psum("ps_tr", [128, 512], F32)
        ps_z = [tc.psum(f"ps_z{j}", [128, 512], F32) for j in range(2)]
        ps_w = tc.psum("ps_w", [16, 512], F32)
        ps_bc = tc.psum("ps_bc", [128, 512], F32)
        ps_ms = tc.psum("ps_ms", [16, 512], F32)
        ps_y = tc.psum("ps_y", [128, 512], F32)

        xr = xn_d.rearrange("(n p) f -> n p f", p=128)
        s0r = s0p.rearrange("(n p) f -> n p f", p=128)
        s0hr = s0h.rearrange("(n p) f -> n p f", p=128)
        outr = out_d.rearrange("(n p) f -> n p f", p=128)

        with nc.Block() as block:
            d_id = S.dma("sync", identt[:], ident_d[:, :])
            d_win = S.dma("gpsimd", wint[:], win_d.rearrange("(c p) f -> p c f", p=128))
            d_wout = S.dma("gpsimd", woutt[:], wout_d.rearrange("(c p) f -> p c f", p=128))
            d_wre = S.dma("gpsimd", wreb[:], wre_d[:, :])
            d_wim = S.dma("gpsimd", wimb[:], wim_d[:, :])
            d_th = S.dma("sync", theta[:], theta_d[:, :])
            d_nsre = S.dma("sync", nsre[:], nsre_d[:, :])
            d_nsim = S.dma("sync", nsim[:], nsim_d[:, :])
            d_b16 = S.dma("gpsimd", b16[:], b16_d[:, :])
            d_bsum = S.dma("sync", bsum[:], bsum_d[:, :])

            d_sel = S.dma("sync", selrow[:], selrow_d[:, :])
            d_ss = S.dma("sync", sscol[:], ss_d[:, :])
            d_sl = S.dma("sync", nspcol[0:16, 0:1], slope_d[:, :])
            d_ck = S.dma("sync", ckraw[:], ck_d[:, :])
            d_ones = S.add("vector", lambda e: e.memset(onescol[:], 1.0))
            d_pi = S.add("vector", lambda e: e.memset(picol[:, 0:1], PI / 2))
            d_eps = S.add("vector", lambda e: e.memset(picol[:, 1:2], 1e-5))
            d_car0 = S.add("vector", lambda e: e.memset(carry[:], 0.0))
            d_card0 = S.add("vector", lambda e: e.memset(carryd[:], 0.0))
            d_z0 = S.add("vector", lambda e: e.memset(zpre[:, :, 0:3], 0.0))

            d_sp1 = S.add("scalar", lambda e: e.activation(out=nspcol[0:16, 1:2],
                                                           in_=nspcol[0:16, 0:1], func=AF.Abs),
                          deps=[d_sl])
            d_sp2 = S.add("scalar", lambda e: e.activation(out=nspcol[0:16, 1:2],
                                                           in_=nspcol[0:16, 1:2], func=AF.Exp,
                                                           scale=-1.0), deps=[d_sp1])
            d_sp3 = S.add("scalar", lambda e: e.activation(out=nspcol[0:16, 1:2],
                                                           in_=nspcol[0:16, 1:2], func=AF.Ln,
                                                           bias=onescol[0:16, 0:1], scale=1.0),
                          deps=[d_sp2, d_ones])
            d_sp4 = S.add("vector", lambda e: e.tensor_scalar_max(nspcol[0:16, 2:3],
                                                                  nspcol[0:16, 0:1], 0.0),
                          deps=[d_sl])
            d_sp5 = S.add("vector", lambda e: e.tensor_tensor(nspcol[0:16, 3:4],
                                                              nspcol[0:16, 1:2],
                                                              nspcol[0:16, 2:3], ALU.add),
                          deps=[d_sp3, d_sp4])
            d_nsp = S.add("vector", lambda e: e.tensor_scalar_mul(nspcol[0:16, 3:4],
                                                                  nspcol[0:16, 3:4], -1.0),
                          deps=[d_sp5])

            prev_tr = d_id
            for c in range(9):
                w = 128 if c < 8 else 16
                d_tr = S.add("tensor", lambda e, c=c, w=w: e.transpose(
                    ps_tr[0:w, 0:4], ckraw[0:4, c * 128:c * 128 + w], identt[0:4, 0:4]),
                    deps=[d_ck, d_id, prev_tr])
                prev_tr = S.add("vector", lambda e, c=c, w=w: e.tensor_copy(
                    ckt[0:w, c, :], ps_tr[0:w, 0:4]), deps=[d_tr])

            prev_ld = {0: None, 1: None}
            for tt in range(SEQ // 128):
                bb = tt % 2
                d_x = S.dma("sync", xt[bb][:], xr[tt, :, :], deps=[prev_ld[bb]],
                            stream=f"xt{bb}")
                ev = None
                for dc in range(NDC):
                    d_t = S.add("tensor", lambda e, dc=dc, bb=bb: e.transpose(
                        ps_tr[:, 0:128], xt[bb][:, dc * 128:(dc + 1) * 128], identt[:]),
                        deps=[d_x, prev_tr])
                    ev = S.add("scalar", lambda e, dc=dc, tt=tt: e.copy(
                        out=xnf[dc][:, tt * 128:(tt + 1) * 128], in_=ps_tr[:, 0:128]),
                        deps=[d_t])
                    prev_tr = ev
                prev_ld[bb] = ev
            d_xnf = prev_tr

            zev = {}
            d_conv = {}
            zcv_war = {zc: None for zc in range(9)}
            csb_war = {g: d_car0 for g in range(8)}
            nrni_war = {g: None for g in range(8)}
            d_store = []
            last_sevac = d_xnf
            d_pwbf = None
            d_pwf_prev = None
            d_lwmm_prev = None
            d_scan_d = d_card0
            d_carryd_upd = d_card0
            d_scans = {g: d_car0 for g in range(8)}
            d_carry_upd = {g: d_car0 for g in range(8)}
            d_invd = None
            d_mx_reader = None
            d_lw = None
            d_msev = None
            d_rs = None
            d_wkbf = None
            prev_ms_mm = None
            tmp_war = None
            gate_war = None
            opbf_war = {0: None, 1: None}
            ygbf_war = {g: None for g in range(4)}
            psy_war = None
            psbc_war = None

            for tt in range(NT):
                tok0 = tt * 512
                for zc in range(9):
                    w = 128 if zc < 8 else 16
                    pb = zc % 2
                    prev_user = zev.get((tt, zc - 2)) if zc >= 2 else \
                        zev.get((tt - 1, 8 if pb == 0 else 7))
                    mm = None
                    for dc in range(NDC):
                        deps = []
                        if dc == 0:
                            deps = [d_win, d_xnf, prev_user]
                        mm = S.add("tensor", lambda e, zc=zc, dc=dc, w=w, pb=pb, tok0=tok0: e.matmul(
                            ps_z[pb][0:w, 0:512], wint[:, dc, zc * 128:zc * 128 + w],
                            xnf[dc][:, tok0:tok0 + 512],
                            start=(dc == 0), stop=(dc == NDC - 1)), deps=deps)
                    if tt > 0:
                        d_tail = S.add("vector", lambda e, zc=zc, w=w: e.tensor_copy(
                            zpre[0:w, zc, 0:3], zpre[0:w, zc, 512:515]),
                            deps=[zev.get((tt - 1, zc)), d_conv.get((tt - 1, zc))])
                    else:
                        d_tail = d_z0
                    zev[(tt, zc)] = S.add("scalar", lambda e, zc=zc, w=w, pb=pb: e.copy(
                        out=zpre[0:w, zc, 3:515], in_=ps_z[pb][0:w, 0:512]),
                        deps=[mm, d_tail, d_conv.get((tt - 1, zc))])
                    d_c0 = S.add("scalar", lambda e, zc=zc, w=w: e.activation(
                        out=zcv[0:w, zc, :], in_=zpre[0:w, zc, 0:512], func=AF.Copy,
                        scale=ckt[0:w, zc, 0:1]),
                        deps=[zev[(tt, zc)], zcv_war[zc], prev_tr])
                    dd = d_c0
                    for j in range(1, 4):
                        dd = S.add("vector", lambda e, zc=zc, w=w, j=j: e.scalar_tensor_tensor(
                            out=zcv[0:w, zc, :], in0=zpre[0:w, zc, j:512 + j],
                            scalar=ckt[0:w, zc, j:j + 1], in1=zcv[0:w, zc, :],
                            op0=ALU.mult, op1=ALU.add), deps=[dd, zev[(tt, zc)]])
                    d_conv[(tt, zc)] = dd

                d_pd = S.dma("sync", posdist[:], posdist_d[:, tok0:tok0 + 512],
                             deps=[d_lwmm_prev], stream="pd")
                d_lwmm = S.add("tensor", lambda e: e.matmul(
                    ps_w[0:16, 0:512], selrow[:, :], posdist[:, 0:512],
                    start=True, stop=True), deps=[d_sel, d_pd, d_lw])
                d_lwmm_prev = d_lwmm
                d_lw = S.add("vector", lambda e: e.tensor_scalar(
                    pwt[0:16, 0, :], ps_w[0:16, 0:512], nspcol[0:16, 3:4], None,
                    op0=ALU.mult), deps=[d_lwmm, d_nsp, d_pwf_prev])
                d_A = S.add("vector", lambda e: e.scalar_tensor_tensor(
                    out=pwt[0:16, 1, :], in0=zcv[0:16, 8, :], scalar=sscol[0:16, 0:1],
                    in1=pwt[0:16, 0, :], op0=ALU.mult, op1=ALU.add),
                    deps=[d_conv[(tt, 8)], d_lw, d_ss, d_pwf_prev])
                zcv_war[8] = d_A
                d_pwf = S.add("scalar", lambda e: e.activation(
                    out=pwt[0:16, 2, :], in_=pwt[0:16, 1, :], func=AF.Exp),
                    deps=[d_A, d_scan_d, d_pwbf])
                d_pwf_prev = d_pwf
                d_pwbf = S.add("vector", lambda e: e.tensor_copy(
                    pwbf[0:16, :], pwt[0:16, 2, :]), deps=[d_pwf, psbc_war])
                d_scan_d = S.add("vector", lambda e: e.tensor_tensor_scan(
                    out=csd[0:16, :], data0=onescol[0:16, 0:512], data1=pwt[0:16, 2, :],
                    initial=carryd[0:16, 0:1], op0=ALU.mult, op1=ALU.add),
                    deps=[d_pwf, d_ones, d_carryd_upd, d_mx_reader])
                d_carryd_upd = S.add("vector", lambda e: e.tensor_copy(
                    carryd[0:16, 0:1], csd[0:16, 511:512]), deps=[d_scan_d])
                d_mx = S.add("vector", lambda e: e.tensor_scalar_max(
                    pwt[0:16, 3, :], csd[0:16, :], 1e-4),
                    deps=[d_scan_d, d_wkbf])
                d_mx_reader = d_mx
                d_invd = S.add("vector", lambda e: e.reciprocal(
                    pwt[0:16, 3, :], pwt[0:16, 3, :]), deps=[d_mx])

                d_ni_last = None
                for g in range(4):
                    d_p1 = S.add("scalar", lambda e, g=g: e.activation(
                        out=tmp1[:], in_=zcv[:, g, :], func=AF.Copy,
                        scale=theta[:, g:g + 1]),
                        deps=[d_conv[(tt, g)], d_th, d_ni_last, tmp_war, prev_ms_mm])
                    zcv_war[g] = d_p1
                    d_p2 = S.add("vector", lambda e: e.tensor_scalar(
                        tmp2[:], tmp1[:], 1.0 / (2 * PI), MAGIC, op0=ALU.mult,
                        op1=ALU.add), deps=[d_p1, prev_ms_mm])
                    d_p3 = S.add("vector", lambda e: e.tensor_scalar(
                        tmp2[:], tmp2[:], MAGIC, None, op0=ALU.subtract), deps=[d_p2])
                    d_p4 = S.add("vector", lambda e: e.scalar_tensor_tensor(
                        out=tmp1[:], in0=tmp2[:], scalar=-2 * PI, in1=tmp1[:],
                        op0=ALU.mult, op1=ALU.add), deps=[d_p3])
                    d_ab = S.add("scalar", lambda e: e.activation(
                        out=tmp3[:], in_=tmp1[:], func=AF.Abs), deps=[d_p4])
                    d_cos = S.add("scalar", lambda e: e.activation(
                        out=cosv[:], in_=tmp3[:], func=AF.Sin, bias=picol[:, 0:1],
                        scale=-1.0), deps=[d_ab, d_pi])
                    d_sin = S.add("scalar", lambda e: e.activation(
                        out=tmp3[:], in_=tmp1[:], func=AF.Sin), deps=[d_cos])
                    d_bc = S.add("tensor", lambda e, g=g: e.matmul(
                        ps_bc[:, 0:512], b16[:, g * 128:(g + 1) * 128], pwbf[0:16, :],
                        start=True, stop=True),
                        deps=[d_b16, d_pwbf, psbc_war])
                    d_nr = S.add("vector", lambda e, g=g: e.tensor_tensor(
                        nrni[:, g, :], cosv[:], ps_bc[:, 0:512], ALU.mult),
                        deps=[d_bc, d_cos, nrni_war[g]])
                    d_ni = S.add("vector", lambda e, g=g: e.tensor_tensor(
                        nrni[:, 4 + g, :], tmp3[:], ps_bc[:, 0:512], ALU.mult),
                        deps=[d_bc, d_sin, nrni_war[4 + g], d_nr])
                    d_ni_last = d_ni
                    psbc_war = d_ni
                    tmp_war = d_ni
                    d_sre = S.add("vector", lambda e, g=g: e.tensor_tensor_scan(
                        out=csb[:, g, :], data0=onescol[:, 0:512], data1=nrni[:, g, :],
                        initial=carry[:, g:g + 1], op0=ALU.mult, op1=ALU.add),
                        deps=[d_nr, d_carry_upd[g], csb_war[g]])
                    nrni_war[g] = d_sre
                    d_cu1 = S.add("vector", lambda e, g=g: e.tensor_copy(
                        carry[:, g:g + 1], csb[:, g, 511:512]), deps=[d_sre])
                    d_sim = S.add("vector", lambda e, g=g: e.tensor_tensor_scan(
                        out=csb[:, 4 + g, :], data0=onescol[:, 0:512],
                        data1=nrni[:, 4 + g, :],
                        initial=carry[:, 4 + g:5 + g], op0=ALU.mult, op1=ALU.add),
                        deps=[d_ni, d_carry_upd[4 + g], csb_war[4 + g], d_cu1])
                    nrni_war[4 + g] = d_sim
                    d_cu2 = S.add("vector", lambda e, g=g: e.tensor_copy(
                        carry[:, 4 + g:5 + g], csb[:, 4 + g, 511:512]), deps=[d_sim])
                    d_scans[g], d_scans[4 + g] = d_sre, d_sim
                    d_carry_upd[g], d_carry_upd[4 + g] = d_cu1, d_cu2

                last_ms = None
                for gg in range(8):
                    d_sq = S.add("scalar", lambda e, gg=gg: e.activation(
                        out=tmp2[:], in_=csb[:, gg, :], func=AF.Square),
                        deps=[d_scans[gg], last_ms, tmp_war, d_msev])
                    tmp_war = d_sq
                    g4 = gg % 4
                    last_ms = S.add("tensor", lambda e, gg=gg, g4=g4: e.matmul(
                        ps_ms[0:16, 0:512], bsum[:, g4 * 16:(g4 + 1) * 16], tmp2[:],
                        start=(gg == 0), stop=(gg == 7)),
                        deps=[d_sq, d_bsum] + ([d_msev] if gg == 0 else []))
                    csb_war[gg] = d_sq
                prev_ms_mm = last_ms
                d_w2 = S.add("vector", lambda e: e.tensor_tensor(
                    rsb[0:16, :], pwt[0:16, 3, :], pwt[0:16, 3, :], ALU.mult),
                    deps=[d_invd, d_rs])
                d_ms1 = S.add("vector", lambda e: e.tensor_tensor(
                    rsb[0:16, :], ps_ms[0:16, 0:512], rsb[0:16, :], ALU.mult),
                    deps=[last_ms, d_w2])
                d_msev = d_ms1
                d_lnr = S.add("scalar", lambda e: e.activation(
                    out=rsb[0:16, :], in_=rsb[0:16, :], func=AF.Ln,
                    bias=picol[0:16, 1:2], scale=1.0 / 64), deps=[d_ms1, d_eps])
                d_rs = S.add("scalar", lambda e: e.activation(
                    out=rsb[0:16, :], in_=rsb[0:16, :], func=AF.Exp, scale=-0.5),
                    deps=[d_lnr])
                d_wk = S.add("vector", lambda e: e.tensor_tensor(
                    pwt[0:16, 3, :], pwt[0:16, 3, :], rsb[0:16, :], ALU.mult),
                    deps=[d_rs, d_wkbf])
                d_wkbf = S.add("vector", lambda e: e.tensor_copy(
                    wkbf[0:16, :], pwt[0:16, 3, :]), deps=[d_wk])

                d_yg = {}
                for g in range(4):
                    d_bcw = S.add("tensor", lambda e, g=g: e.matmul(
                        ps_bc[:, 0:512], b16[:, g * 128:(g + 1) * 128], wkbf[0:16, :],
                        start=True, stop=True),
                        deps=[d_wkbf, psbc_war])
                    d_ore = S.add("vector", lambda e, g=g: e.scalar_tensor_tensor(
                        out=opbf[:, 0, :], in0=csb[:, g, :], scalar=nsre[:, 0:1],
                        in1=ps_bc[:, 0:512], op0=ALU.mult, op1=ALU.mult),
                        deps=[d_bcw, d_nsre, d_scans[g], opbf_war[0]])
                    csb_war[g] = d_ore
                    d_oim = S.add("vector", lambda e, g=g: e.scalar_tensor_tensor(
                        out=opbf[:, 1, :], in0=csb[:, 4 + g, :], scalar=nsim[:, 0:1],
                        in1=ps_bc[:, 0:512], op0=ALU.mult, op1=ALU.mult),
                        deps=[d_bcw, d_nsim, d_scans[4 + g], opbf_war[1], d_ore])
                    csb_war[4 + g] = d_oim
                    psbc_war = d_oim
                    d_ymm1 = S.add("tensor", lambda e: e.matmul(
                        ps_y[:, 0:512], wreb[:], opbf[:, 0, :], start=True, stop=False),
                        deps=[d_ore, d_wre, psy_war])
                    d_ymm = S.add("tensor", lambda e: e.matmul(
                        ps_y[:, 0:512], wimb[:], opbf[:, 1, :], start=False, stop=True),
                        deps=[d_oim, d_wim])
                    opbf_war = {0: d_ymm, 1: d_ymm}
                    d_gt = S.add("scalar", lambda e, g=g: e.activation(
                        out=gatev[:], in_=zcv[:, 4 + g, :], func=AF.Silu),
                        deps=[d_conv[(tt, 4 + g)], gate_war])
                    zcv_war[4 + g] = d_gt
                    d_yg[g] = S.add("vector", lambda e, g=g: e.tensor_tensor(
                        ygbf[:, g, :], gatev[:], ps_y[:, 0:512], ALU.mult),
                        deps=[d_ymm, d_gt, ygbf_war[g]])
                    gate_war = d_yg[g]
                    psy_war = d_yg[g]

                for ts4 in range(4):
                    for dh in range(2):
                        mm = None
                        for g in range(4):
                            deps = []
                            if g == 0:
                                deps = [d_yg[0], d_yg[1], d_yg[2], d_yg[3],
                                        d_wout, last_sevac]
                            mm = S.add("tensor", lambda e, ts4=ts4, dh=dh, g=g: e.matmul(
                                ps_tr[:, 0:512], ygbf[:, g, ts4 * 128:(ts4 + 1) * 128],
                                woutt[:, g, dh * 512:(dh + 1) * 512],
                                start=(g == 0), stop=(g == 3)), deps=deps)
                        last_sevac = S.add("vector", lambda e, dh=dh: e.tensor_copy(
                            sevac[:, dh * 512:(dh + 1) * 512], ps_tr[:, 0:512]),
                            deps=[mm, d_store[-1] if d_store else None])
                        if dh == 1:
                            d_st = S.dma("gpsimd", s0r[tt * 4 + ts4, :, :], sevac[:, :],
                                         deps=[last_sevac], stream="s0st")
                            d_store.append(d_st)
                for g in range(4):
                    ygbf_war[g] = d_store[-1]

            d_rsc = S.add("gpsimd", lambda e: e.collective_compute(
                "ReduceScatter", ALU.add, replica_groups=PAIRS,
                ins=[s0p.ap().opt()], outs=[s0h.ap().opt()]),
                deps=d_store)
            for i in range(HALF // 128):
                S.dma("gpsimd", outr[i, :, :], s0hr[i, :, :], deps=[d_rsc],
                      stream="outst")
            S.finalize(nc, es, block)
    return nc


def l0attn_inputs(g, h):
    k0 = 16 * h
    cols = np.concatenate([
        np.arange(32 * k0, 32 * k0 + 512),
        1024 + np.arange(512 * h, 512 * h + 512),
        2048 + np.arange(k0, k0 + 16),
    ])
    W_in = np.asarray(g["L0_attn_W_in"])
    win = np.ascontiguousarray(W_in[:, cols], np.float32)
    ck = np.ascontiguousarray(np.asarray(g["L0_attn_conv_kernel"])[:, 0, cols], np.float32)
    theta = np.asarray(g["L0_attn_theta"]).reshape(32, 32)[k0:k0 + 16].reshape(512)
    theta_in = np.ascontiguousarray(theta.reshape(4, 128).T, np.float32)
    slopes = np.concatenate([np.asarray(g["L0_attn_decay_slopes"]),
                             np.asarray(g["L0_attn_anchor_slopes"])])
    slope = np.ascontiguousarray(slopes[k0:k0 + 16].reshape(16, 1), np.float32)
    selrow = np.zeros((2, 16), np.float32)
    for kl in range(16):
        if k0 + kl >= 28:
            selrow[1, kl] = 1.0
        else:
            selrow[0, kl] = 1.0
    ar = np.arange(L, dtype=np.float32)
    posdist = np.ascontiguousarray(np.stack([np.float32(L - 1) - ar, ar]))
    ss = np.ascontiguousarray(np.asarray(g["L0_attn_score_scale"])[k0:k0 + 16]
                              .reshape(16, 1), np.float32)
    W_re = np.asarray(g["L0_attn_W_re"], np.float32)
    W_im = np.asarray(g["L0_attn_W_im"], np.float32)
    wre_bd = np.zeros((128, 128), np.float32)
    wim_bd = np.zeros((128, 128), np.float32)
    for j in range(4):
        wre_bd[j * 32:(j + 1) * 32, j * 32:(j + 1) * 32] = W_re
        wim_bd[j * 32:(j + 1) * 32, j * 32:(j + 1) * 32] = W_im
    ns = np.asarray(g["L0_attn_norm_scale"], np.float32)
    nsre = np.ascontiguousarray(np.tile(ns[:32], 4).reshape(128, 1))
    nsim = np.ascontiguousarray(np.tile(ns[32:], 4).reshape(128, 1))
    wout = np.ascontiguousarray(np.asarray(g["L0_attn_W_out"])[32 * k0:32 * k0 + 512],
                                np.float32)
    b16 = np.zeros((16, 512), np.float32)
    for j in range(512):
        b16[j // 32, j] = 1.0
    bsum = np.zeros((128, 64), np.float32)
    for g4 in range(4):
        for p in range(128):
            bsum[p, g4 * 16 + 4 * g4 + p // 32] = 1.0
    ident = np.eye(128, dtype=np.float32)
    return {"win": win, "ck": ck, "theta": theta_in, "slope": slope,
            "selrow": selrow, "posdist": posdist, "ss": ss, "wre": wre_bd,
            "wim": wim_bd, "nsre": nsre, "nsim": nsim, "wout": wout,
            "b16": b16, "bsum": bsum, "ident": ident}


def build_l1attn():
    nc = bass.Bass("TRN2", target_bir_lowering=False)
    SEQ = L
    NQT = SEQ // 128
    NDC = D // 128
    xn_d = nc.dram_tensor("xn", [SEQ, D], F32, kind="ExternalInput").ap()
    xnv_d = nc.dram_tensor("xnv", [SEQ, 512], F32, kind="ExternalInput").ap()
    wqk_d = nc.dram_tensor("wqk", [D, 8 * 128], F32, kind="ExternalInput").ap()
    efm_d = nc.dram_tensor("efm", [512, D], F32, kind="ExternalInput").ap()
    cmask_d = nc.dram_tensor("cmask", [128, 4 * 512], F32, kind="ExternalInput").ap()
    invpos_d = nc.dram_tensor("invpos", [1, SEQ], F32, kind="ExternalInput").ap()
    abg_d = nc.dram_tensor("abg", [1, 3], F32, kind="ExternalInput").ap()
    ident_d = nc.dram_tensor("ident", [128, 128], F32, kind="ExternalInput").ap()
    out_d = nc.dram_tensor("out", [D, HALF], F32, kind="ExternalOutput").ap()

    s1p = nc.dram_tensor("s1p", [2 * D, HALF], F32)
    s1h = nc.dram_tensor("s1h", [D, HALF], F32)
    PAIRS = [[0, 1], [2, 3], [4, 5], [6, 7]]

    es = contextlib.ExitStack()
    with es:
        tc = TileCtx(nc, es)
        S = Sched()
        xnf = [tc.sbuf(f"xnf{dc}", [128, SEQ], BF16) for dc in range(NDC)]
        wqkt = tc.sbuf("wqkt", [128, NDC, 8 * 128], BF16)
        efmt = tc.sbuf("efmt", [128, 4, D], BF16)
        cmaskt = tc.sbuf("cmaskt", [128, 4, 512], F32)
        invpos = tc.sbuf("invpost", [1, SEQ], F32)
        identt = tc.sbuf("identt", [128, 128], F32)
        identb = tc.sbuf("identb", [128, 128], BF16)
        abg = tc.sbuf("abgt", [1, 3], F32)
        acol = tc.sbuf("acolt", [128, 3], F32)
        xt = [tc.sbuf(f"xt{j}", [128, D], F32) for j in range(2)]
        qf = tc.sbuf("qf", [64, SEQ], BF16)
        kf = tc.sbuf("kf", [64, SEQ], BF16)
        vtm = tc.sbuf("vtm", [128, NQT, 64], BF16)
        attf = [tc.sbuf(f"attf{j}", [128, SEQ], BF16) for j in range(4)]
        ptb = tc.sbuf("ptb", [128, 512], BF16)
        pttb = tc.sbuf("pttb", [128, 512], BF16)
        accb = tc.sbuf("accb", [128, 64], F32)
        ybf = tc.sbuf("ybf", [128, 64], BF16)
        cols = tc.sbuf("colsb", [128, 8], F32)
        csrow = tc.sbuf("csrow", [128, 512], F32)
        carrc = tc.sbuf("carrc", [128, 1], F32)
        termv = tc.sbuf("termv", [128, 512], F32)
        termv2 = tc.sbuf("termv2", [128, 512], F32)
        sevac = tc.sbuf("sevact", [128, 512], F32)
        onesr = tc.sbuf("onesrt", [1, 128], F32)
        onesf = tc.sbuf("onesf", [128, 512], F32)

        ps_tr = tc.psum("ps_tr", [128, 512], F32)
        ps_s = tc.psum("ps_s", [128, 512], F32)
        ps_pt = tc.psum("ps_pt", [128, 512], BF16)
        ps_pv = tc.psum("ps_pv", [128, 64], F32)
        ps_qk = tc.psum("ps_qk", [128, 512], F32)
        ps_pl = tc.psum("ps_pl", [128, 512], F32)
        ps_bc = tc.psum("ps_bc", [128, 512], F32)

        xr = xn_d.rearrange("(n p) f -> n p f", p=128)
        s1pr = s1p.rearrange("(n p) f -> n p f", p=128)
        s1hr = s1h.rearrange("(n p) f -> n p f", p=128)
        outr = out_d.rearrange("(n p) f -> n p f", p=128)

        with nc.Block() as block:
            d_id = S.dma("sync", identt[:], ident_d[:, :])
            d_idb = S.dma("gpsimd", identb[:], ident_d[:, :])
            d_wqk = S.dma("gpsimd", wqkt[:], wqk_d.rearrange("(c p) f -> p c f", p=128))
            d_efm = S.dma("gpsimd", efmt[:], efm_d.rearrange("(c p) f -> p c f", p=128))
            d_cm = S.dma("sync", cmaskt[:], cmask_d.rearrange("p (a b) -> p a b", a=4))
            d_ip = S.dma("sync", invpos[:], invpos_d[:, :])
            d_ab = S.dma("sync", abg[:], abg_d[:, :])
            d_ones = S.add("vector", lambda e: e.memset(onesr[:], 1.0))
            d_onesf = S.add("vector", lambda e: e.memset(onesf[:], 1.0))
            prev = d_id
            for i in range(3):
                prev = S.add("tensor", lambda e, i=i, prev=None: e.matmul(
                    ps_bc[:, i:i + 1], onesr[:], abg[0:1, i:i + 1],
                    start=True, stop=True), deps=[d_ones, d_ab, prev])
            d_ac = S.add("vector", lambda e: e.tensor_copy(acol[:, 0:3], ps_bc[:, 0:3]),
                         deps=[prev])
            d_ah = S.add("vector", lambda e: e.tensor_scalar_mul(acol[:, 0:1],
                                                                 acol[:, 0:1], 0.5),
                         deps=[d_ac])
            d_gh = S.add("vector", lambda e: e.tensor_scalar_mul(acol[:, 2:3],
                                                                 acol[:, 2:3], -0.5),
                         deps=[d_ah])

            prev_tr = d_id
            prev_ld = {0: None, 1: None}
            d_bc_war = d_ac
            for tt in range(SEQ // 128):
                bb = tt % 2
                d_x = S.dma("sync", xt[bb][:], xr[tt, :, :], deps=[prev_ld[bb]],
                            stream=f"xt{bb}")
                ev = None
                for dc in range(NDC):
                    d_t = S.add("tensor", lambda e, dc=dc, bb=bb: e.transpose(
                        ps_tr[:, 0:128], xt[bb][:, dc * 128:(dc + 1) * 128], identt[:]),
                        deps=[d_x, prev_tr])
                    ev = S.add("scalar", lambda e, dc=dc, tt=tt: e.copy(
                        out=xnf[dc][:, tt * 128:(tt + 1) * 128], in_=ps_tr[:, 0:128]),
                        deps=[d_t])
                    prev_tr = ev
                prev_ld[bb] = ev
            d_xnf = prev_tr

            d_attev = None
            qk_ev = None
            d_vt = None
            ps_s_war = None
            ps_pt_ev = None
            acc_set = None
            y_done = None
            for j in range(8):
                d_vt = S.dma("gpsimd", vtm[:],
                             xnv_d.rearrange("(n p) f -> p n f", p=128)[:, :, j * 64:(j + 1) * 64],
                             deps=[y_done], stream="vtm")
                for tt in range(SEQ // 512):
                    mm = None
                    for dc in range(NDC):
                        deps = [d_wqk, d_xnf, qk_ev] if dc == 0 else []
                        mm = S.add("tensor", lambda e, j=j, tt=tt, dc=dc: e.matmul(
                            ps_qk[:, 0:512], wqkt[:, dc, j * 128:(j + 1) * 128],
                            xnf[dc][:, tt * 512:(tt + 1) * 512],
                            start=(dc == 0), stop=(dc == NDC - 1)), deps=deps)
                    d_q = S.add("scalar", lambda e, tt=tt: e.activation(
                        out=qf[0:64, tt * 512:(tt + 1) * 512], in_=ps_qk[0:64, 0:512],
                        func=AF.Copy, scale=0.125), deps=[mm])
                    qk_ev = S.add("scalar", lambda e, tt=tt: e.copy(
                        out=kf[0:64, tt * 512:(tt + 1) * 512], in_=ps_qk[64:128, 0:512]),
                        deps=[d_q])
                for qt in range(NQT):
                    nkc = qt // 4 + 1
                    for kc in range(nkc):
                        d_smm = S.add("tensor", lambda e, qt=qt, kc=kc: e.matmul(
                            ps_s[:, 0:512], qf[0:64, qt * 128:(qt + 1) * 128],
                            kf[0:64, kc * 512:(kc + 1) * 512],
                            start=True, stop=True),
                            deps=[qk_ev, ps_s_war])
                        d_sc = d_smm
                        if kc == qt // 4:
                            d_sc = S.add("vector", lambda e, qt=qt: e.tensor_tensor(
                                ps_s[:, 0:512], ps_s[:, 0:512],
                                cmaskt[:, qt % 4, :], ALU.add), deps=[d_sc, d_cm])
                        d_mx = S.add("vector", lambda e: e.reduce_max(
                            cols[:, 2:3], ps_s[:, 0:512], axis=AX.X), deps=[d_sc])
                        if kc == 0:
                            d_m2 = S.add("vector", lambda e: e.tensor_copy(
                                cols[:, 0:1], cols[:, 2:3]), deps=[d_mx, y_done])
                            d_nm = S.add("vector", lambda e: e.tensor_scalar_mul(
                                cols[:, 3:4], cols[:, 0:1], -1.0), deps=[d_m2])
                            d_p = S.add("scalar", lambda e: e.activation(
                                out=ptb[:], in_=ps_s[:, 0:512], func=AF.Exp,
                                bias=cols[:, 3:4], scale=1.0,
                                accum_out=cols[:, 1:2]),
                                deps=[d_nm, ps_pt_ev])
                        else:
                            d_m2 = S.add("vector", lambda e: e.tensor_max(
                                cols[:, 4:5], cols[:, 0:1], cols[:, 2:3]),
                                deps=[d_mx, acc_set])
                            d_fd = S.add("vector", lambda e: e.tensor_tensor(
                                cols[:, 5:6], cols[:, 0:1], cols[:, 4:5], ALU.subtract),
                                deps=[d_m2])
                            d_f = S.add("scalar", lambda e: e.activation(
                                out=cols[:, 5:6], in_=cols[:, 5:6], func=AF.Exp),
                                deps=[d_fd])
                            d_nm = S.add("vector", lambda e: e.tensor_scalar_mul(
                                cols[:, 3:4], cols[:, 4:5], -1.0), deps=[d_m2])
                            d_cp = S.add("vector", lambda e: e.tensor_copy(
                                cols[:, 0:1], cols[:, 4:5]), deps=[d_nm])
                            d_p = S.add("scalar", lambda e: e.activation(
                                out=ptb[:], in_=ps_s[:, 0:512], func=AF.Exp,
                                bias=cols[:, 3:4], scale=1.0,
                                accum_out=cols[:, 6:7]),
                                deps=[d_nm, d_f, ps_pt_ev])
                            d_sr = S.add("vector", lambda e: e.scalar_tensor_tensor(
                                out=cols[:, 1:2], in0=cols[:, 1:2], scalar=cols[:, 5:6],
                                in1=cols[:, 6:7], op0=ALU.mult, op1=ALU.add),
                                deps=[d_p])
                        ps_s_war = d_p
                        tr_prev = ps_pt_ev
                        d_trp = None
                        for c4 in range(4):
                            d_trp = S.add("tensor", lambda e, c4=c4: e.transpose(
                                ps_pt[:, c4 * 128:(c4 + 1) * 128],
                                ptb[:, c4 * 128:(c4 + 1) * 128], identb[:]),
                                deps=[d_p, d_idb, tr_prev])
                            tr_prev = d_trp
                        ps_pt_ev = S.add("vector", lambda e: e.tensor_copy(
                            pttb[:], ps_pt[:, 0:512]), deps=[d_trp, acc_set])
                        pv_mm = None
                        for c4 in range(4):
                            deps = [ps_pt_ev, d_vt]
                            if c4 == 0:
                                deps.append(acc_set)
                            pv_mm = S.add("tensor", lambda e, qt=qt, kc=kc, c4=c4: e.matmul(
                                ps_pv[:, 0:64], pttb[:, c4 * 128:(c4 + 1) * 128],
                                vtm[:, kc * 4 + c4, :],
                                start=(c4 == 0), stop=(c4 == 3)), deps=deps)
                        if kc == 0:
                            acc_set = S.add("vector", lambda e: e.tensor_copy(
                                accb[:], ps_pv[:, 0:64]), deps=[pv_mm, y_done])
                        else:
                            acc_set = S.add("vector", lambda e: e.scalar_tensor_tensor(
                                out=accb[:], in0=accb[:], scalar=cols[:, 5:6],
                                in1=ps_pv[:, 0:64], op0=ALU.mult, op1=ALU.add),
                                deps=[pv_mm, d_sr, acc_set])
                    d_rc = S.add("vector", lambda e: e.reciprocal(cols[:, 7:8],
                                                                  cols[:, 1:2]),
                                 deps=[acc_set])
                    d_rb = S.add("vector", lambda e: e.tensor_tensor(
                        cols[:, 7:8], cols[:, 7:8], acol[:, 1:2], ALU.mult),
                        deps=[d_rc, d_gh])
                    d_y = S.add("vector", lambda e: e.tensor_scalar(
                        ybf[:], accb[:], cols[:, 7:8], None, op0=ALU.mult),
                        deps=[d_rb, y_done])
                    d_try = S.add("tensor", lambda e: e.transpose(
                        ps_pt[0:64, 0:128], ybf[:], identb[:]),
                        deps=[d_y, ps_pt_ev, d_idb])
                    jc, jo = j // 2, (j % 2) * 64
                    y_done = S.add("scalar", lambda e, jc=jc, jo=jo, qt=qt: e.copy(
                        out=attf[jc][jo:jo + 64, qt * 128:(qt + 1) * 128],
                        in_=ps_pt[0:64, 0:128]), deps=[d_try])
                    ps_pt_ev = y_done

            d_stores = []
            d_carset = None
            d_term_done = None
            ps_pl_war = d_xnf
            ps_bc_war2 = d_ac
            for gc in range(NDC):
                d_carset = S.add("vector", lambda e: e.memset(carrc[:], 0.0),
                                 deps=[d_carset])
                d_scan = None
                for tt in range(SEQ // 512):
                    mm = None
                    for lc in range(4):
                        deps = []
                        if lc == 0:
                            deps = [d_efm, y_done, ps_pl_war]
                        mm = S.add("tensor", lambda e, gc=gc, lc=lc, tt=tt: e.matmul(
                            ps_pl[:, 0:512], efmt[:, lc, gc * 128:(gc + 1) * 128],
                            attf[lc][:, tt * 512:(tt + 1) * 512],
                            start=(lc == 0), stop=(lc == 3)), deps=deps)
                    d_bc2 = S.add("tensor", lambda e, tt=tt: e.matmul(
                        ps_bc[:, 0:512], onesr[:], invpos[:, tt * 512:(tt + 1) * 512],
                        start=True, stop=True), deps=[d_ones, d_ip, ps_bc_war2])
                    d_scan = S.add("vector", lambda e, gc=gc, tt=tt: e.tensor_tensor_scan(
                        out=csrow[:], data0=onesf[:],
                        data1=xnf[gc][:, tt * 512:(tt + 1) * 512],
                        initial=carrc[:, 0:1], op0=ALU.mult, op1=ALU.add),
                        deps=[d_onesf, d_carset, d_xnf, ps_bc_war2])
                    d_car = S.add("vector", lambda e: e.tensor_copy(
                        carrc[:, 0:1], csrow[:, 511:512]), deps=[d_scan])
                    d_carset = d_car
                    d_t3 = S.add("vector", lambda e, gc=gc, tt=tt: e.tensor_scalar(
                        termv[:], xnf[gc][:, tt * 512:(tt + 1) * 512], acol[:, 0:1],
                        None, op0=ALU.mult), deps=[d_ah, d_scan, d_term_done])
                    d_t4 = S.add("vector", lambda e: e.tensor_tensor(
                        termv2[:], csrow[:], ps_bc[:, 0:512], ALU.mult),
                        deps=[d_scan, d_bc2, d_car, d_term_done])
                    ps_bc_war2 = d_t4
                    d_tm = S.add("vector", lambda e: e.scalar_tensor_tensor(
                        out=sevac[:], in0=termv2[:], scalar=acol[:, 2:3],
                        in1=termv[:], op0=ALU.mult, op1=ALU.add),
                        deps=[d_t4, d_t3, d_gh, d_stores[-1] if d_stores else None])
                    d_fin = S.add("vector", lambda e: e.tensor_tensor(
                        sevac[:], sevac[:], ps_pl[:, 0:512], ALU.add),
                        deps=[d_tm, mm])
                    ps_pl_war = d_fin
                    d_term_done = d_tm
                    d_st = S.dma("gpsimd", s1pr[(tt // 4) * 8 + gc, :,
                                               (tt % 4) * 512:(tt % 4) * 512 + 512],
                                 sevac[:], deps=[d_fin], stream="s1st")
                    d_stores.append(d_st)

            d_rsc = S.add("gpsimd", lambda e: e.collective_compute(
                "ReduceScatter", ALU.add, replica_groups=PAIRS,
                ins=[s1p.ap().opt()], outs=[s1h.ap().opt()]),
                deps=d_stores)
            for i in range(D // 128):
                S.dma("gpsimd", outr[i, :, :], s1hr[i, :, :], deps=[d_rsc],
                      stream="outst")
            S.finalize(nc, es, block)
    return nc


def l1attn_inputs(g, h):
    W = np.asarray(g["L1_attn_W_attn"], np.float32)
    wqk = np.zeros((D, 8 * 128), np.float32)
    for j in range(8):
        gh = 8 * h + j
        wqk[:, j * 128:j * 128 + 64] = W[64 * gh:64 * gh + 64, :].T
        wqk[:, j * 128 + 64:j * 128 + 128] = W[1024 + 64 * gh:1024 + 64 * gh + 64, :].T
    efm = np.zeros((512, D), np.float32)
    for lc in range(512):
        efm[lc, 512 * h + lc] = 1.0
    cmask = np.zeros((128, 4, 512), np.float32)
    for off_i in range(4):
        off = off_i * 128
        for i in range(128):
            cmask[i, off_i, off + i + 1:] = -30000.0
    ar = np.arange(L, dtype=np.float32)
    invpos = (1.0 / (ar + 1.0)).reshape(1, L)
    abg = np.stack([np.float32(np.asarray(g["L1_attn_alpha"])),
                    np.float32(np.asarray(g["L1_attn_beta"])),
                    np.float32(np.asarray(g["L1_attn_gamma"]))]).reshape(1, 3)
    return {"wqk": wqk, "efm": efm, "cmask": cmask.reshape(128, 4 * 512),
            "invpos": invpos, "abg": abg.astype(np.float32),
            "ident": np.eye(128, dtype=np.float32)}


def build_res(s_is_fm, with_ln):
    nc = bass.Bass("TRN2", target_bir_lowering=False)
    x_d = nc.dram_tensor("x", [HALF, D], F32, kind="ExternalInput").ap()
    g_d = nc.dram_tensor("g", [HALF, D], F32, kind="ExternalInput").ap()
    if s_is_fm:
        s_d = nc.dram_tensor("s", [D, HALF], F32, kind="ExternalInput").ap()
        ident_d = nc.dram_tensor("ident", [128, 128], F32, kind="ExternalInput").ap()
    else:
        s_d = nc.dram_tensor("s", [HALF, D], F32, kind="ExternalInput").ap()
    xo_d = nc.dram_tensor("xo", [HALF, D], F32, kind="ExternalOutput").ap()
    if with_ln:
        xn_d = nc.dram_tensor("xn", [HALF, D], F32, kind="ExternalOutput").ap()

    NTT = HALF // 128
    es = contextlib.ExitStack()
    with es:
        tc = TileCtx(nc, es)
        S = Sched()
        xtl = [tc.sbuf(f"xtl{j}", [128, D], F32) for j in range(2)]
        stl = [tc.sbuf(f"stl{j}", [128, D], F32) for j in range(2)]
        gtl = [tc.sbuf(f"gtl{j}", [128, D], F32) for j in range(2)]
        xntl = [tc.sbuf(f"xntl{j}", [128, D], F32) for j in range(2)]
        scratch = tc.sbuf("scratch", [128, D], F32)
        colsr = tc.sbuf("colsr", [128, 4], F32)
        epsc = tc.sbuf("epsc", [128, 1], F32)
        if s_is_fm:
            identt = tc.sbuf("identt", [128, 128], F32)
            sfm = tc.sbuf("sfm", [128, 8, 128], F32)
            ps_tr = tc.psum("ps_tr", [128, 128], F32)
        xr = x_d.rearrange("(n p) f -> n p f", p=128)
        gr = g_d.rearrange("(n p) f -> n p f", p=128)
        xor_ = xo_d.rearrange("(n p) f -> n p f", p=128)
        if with_ln:
            xnr = xn_d.rearrange("(n p) f -> n p f", p=128)
        if not s_is_fm:
            sr = s_d.rearrange("(n p) f -> n p f", p=128)

        with nc.Block() as block:
            d_eps = S.add("vector", lambda e: e.memset(epsc[:], 1e-5))
            if s_is_fm:
                d_idr = S.dma("sync", identt[:], ident_d[:, :])
            prev_x = {0: [], 1: []}
            prev_xn = {0: None, 1: None}
            prev_s = {0: None, 1: None}
            prev_g = {0: None, 1: None}
            prev_sil = None
            prev_ln = d_eps
            prev_trr = d_idr if s_is_fm else None
            prev_sfm = None
            for tt in range(NTT):
                bb = tt % 2
                d_x = S.dma("sync", xtl[bb][:], xr[tt, :, :], deps=prev_x[bb],
                            stream=f"x{bb}")
                d_g = S.dma("sync", gtl[bb][:], gr[tt, :, :], deps=[prev_g[bb]],
                            stream=f"g{bb}")
                if s_is_fm:
                    d_s0 = S.dma("gpsimd", sfm[:],
                                 s_d.rearrange("(c p) t -> p c t", p=128)[:, :, tt * 128:(tt + 1) * 128],
                                 deps=[prev_sfm], stream="sfm")
                    ev = None
                    for dc in range(8):
                        d_t = S.add("tensor", lambda e, dc=dc: e.transpose(
                            ps_tr[:], sfm[:, dc, :], identt[:]),
                            deps=[d_s0, d_idr, prev_trr])
                        ev = S.add("scalar", lambda e, dc=dc, bb=bb: e.copy(
                            out=stl[bb][:, dc * 128:(dc + 1) * 128], in_=ps_tr[:]),
                            deps=[d_t, prev_s[bb]])
                        prev_trr = ev
                    d_s = ev
                    prev_sfm = ev
                else:
                    d_s = S.dma("sync", stl[bb][:], sr[tt, :, :], deps=[prev_s[bb]],
                                stream=f"s{bb}")
                d_sil = S.add("scalar", lambda e, bb=bb: e.activation(
                    out=gtl[bb][:], in_=gtl[bb][:], func=AF.Silu), deps=[d_g, prev_sil])
                d_t1 = S.add("vector", lambda e, bb=bb: e.tensor_tensor(
                    stl[bb][:], stl[bb][:], gtl[bb][:], ALU.mult), deps=[d_sil, d_s])
                d_t2 = S.add("vector", lambda e, bb=bb: e.tensor_tensor(
                    xtl[bb][:], xtl[bb][:], stl[bb][:], ALU.add), deps=[d_t1, d_x])
                prev_sil = d_t1
                d_st = S.dma("gpsimd", xor_[tt, :, :], xtl[bb][:], deps=[d_t2],
                             stream=f"xo{bb}")
                prev_x[bb] = [d_st]
                prev_g[bb] = d_t1
                prev_s[bb] = d_t2
                if with_ln:
                    s_xn = emit_ln_tile(S, tc, xtl[bb], xntl[bb], scratch, colsr,
                                        epsc[:, 0:1], 1.0 / D, d_t2,
                                        extra_deps=[prev_ln, d_st, prev_xn[bb]])
                    prev_ln = s_xn
                    d_stn = S.dma("gpsimd", xnr[tt, :, :], xntl[bb][:], deps=[s_xn],
                                  stream=f"xn{bb}")
                    prev_x[bb] = [d_st, s_xn]
                    prev_xn[bb] = d_stn
            S.finalize(nc, es, block)
    return nc


def _cont(a):
    return np.ascontiguousarray(a, np.float32)


def kernel(**inputs):
    g = inputs
    x = np.asarray(g["L0_attn_x"])
    ident = np.eye(128, dtype=np.float32)
    cores = list(range(NCORES))

    def rows(arrs, c):
        s, h = c // 2, c % 2
        return _cont(arrs[s, h * HALF:(h + 1) * HALF])

    nc = build_ln0()
    maps = [{"x": rows(x, c)} for c in cores]
    r = run_bass_kernel_spmd(nc, maps, core_ids=cores)
    xn0 = [r.results[c]["xn"] for c in cores]

    nc = build_l0attn()
    maps = []
    for c in cores:
        s, h = c // 2, c % 2
        m = l0attn_inputs(g, h)
        m["xn"] = _cont(np.concatenate([xn0[2 * s], xn0[2 * s + 1]], axis=0))
        maps.append(m)
    r = run_bass_kernel_spmd(nc, maps, core_ids=cores)
    s0 = [r.results[c]["out"] for c in cores]

    nc = build_moe(16, 4, "Gelu", False)
    maps = [{"xn": _cont(xn0[c]), "ident": ident,
             "w1": np.asarray(g["L0_ffn_w1"]), "b1": np.asarray(g["L0_ffn_b1"]),
             "w2": np.asarray(g["L0_ffn_w2"]), "b2": np.asarray(g["L0_ffn_b2"]),
             "rw": np.asarray(g["L0_ffn_router_w"]),
             "rb": np.asarray(g["L0_ffn_router_b"]).reshape(1, 16)} for c in cores]
    r = run_bass_kernel_spmd(nc, maps, core_ids=cores)
    g0 = [r.results[c]["out"] for c in cores]

    nc = build_res(False, True)
    maps = [{"x": rows(x, c), "s": _cont(s0[c]), "g": _cont(g0[c])} for c in cores]
    r = run_bass_kernel_spmd(nc, maps, core_ids=cores)
    x1 = [r.results[c]["xo"] for c in cores]
    xn1 = [r.results[c]["xn"] for c in cores]

    nc = build_l1attn()
    maps = []
    for c in cores:
        s, h = c // 2, c % 2
        m = l1attn_inputs(g, h)
        xnseq = np.concatenate([xn1[2 * s], xn1[2 * s + 1]], axis=0)
        m["xn"] = _cont(xnseq)
        m["xnv"] = _cont(xnseq[:, 512 * h:512 * h + 512])
        maps.append(m)
    r = run_bass_kernel_spmd(nc, maps, core_ids=cores)
    s1f = [r.results[c]["out"] for c in cores]

    nc = build_moe(8, 2, "Silu", True)
    maps = [{"xn": _cont(xn1[c]), "ident": ident,
             "w1": np.asarray(g["L1_ffn_ew1"]), "b1": np.asarray(g["L1_ffn_eb1"]),
             "w2": np.asarray(g["L1_ffn_ew2"]), "b2": np.asarray(g["L1_ffn_eb2"]),
             "rw1": np.asarray(g["L1_ffn_rw1"]),
             "rb1": np.asarray(g["L1_ffn_rb1"]).reshape(1, D),
             "rw2": np.asarray(g["L1_ffn_rw2"]),
             "rb2": np.asarray(g["L1_ffn_rb2"]).reshape(1, 8)} for c in cores]
    r = run_bass_kernel_spmd(nc, maps, core_ids=cores)
    g1 = [r.results[c]["out"] for c in cores]

    nc = build_res(True, False)
    maps = [{"x": _cont(x1[c]), "s": _cont(s1f[c]), "g": _cont(g1[c]),
             "ident": ident} for c in cores]
    r = run_bass_kernel_spmd(nc, maps, core_ids=cores)
    out = np.empty((B, L, D), np.float32)
    for c in cores:
        s, h = c // 2, c % 2
        out[s, h * HALF:(h + 1) * HALF] = r.results[c]["xo"]
    return out


if __name__ == "__main__":
    pass
